# version 18
# baseline (speedup 1.0000x reference)
import math
from contextlib import ExitStack

import numpy as np

import concourse.bass as bass
import concourse.mybir as mybir
from concourse.bass_utils import run_bass_kernel_spmd

F32 = mybir.dt.float32
AF = mybir.ActivationFunctionType
ALU = mybir.AluOpType
AX = mybir.AxisListType

FULL_CFG = dict(nseq=2, S=2048, D=1024, NH=16, DH=64, E=16, H=128, NRANGE=2)

ENGINES = ("sync", "gpsimd", "tensor", "vector", "scalar")


class Sem:
    def __init__(self, ctx, nc, name):
        self.h = ctx.enter_context(nc.semaphore(name))
        self.v = 0


class View:

    def __init__(self, t, c0, ncols, nparts=128):
        self.t, self.c0, self.n = t, c0, ncols
        self.np = nparts

    def __getitem__(self, idx):
        rows, cols = idx
        if rows == slice(None):
            rows = slice(0, self.np)
        a = self.c0 + (cols.start or 0)
        b = self.c0 + (cols.stop if cols.stop is not None else self.n)
        return self.t[rows, a:b]


class Sched:

    def __init__(self):
        self.steps = []

    def op(self, eng, fn, waits=(), inc=None, amt=1):
        w = {}
        for s, v in waits:
            if v > 0:
                w[s] = max(w.get(s, 0), v)
        if inc is not None:
            inc.v += amt
        self.steps.append((eng, fn, tuple(w.items()), inc, amt))
        return None if inc is None else inc.v

    def emit(self, block):
        for name in ENGINES:
            self._emitter(block, name)

    def _emitter(self, block, name):
        dec = getattr(block, name)
        steps = [s for s in self.steps if s[0] == name]

        @dec
        def _(eng):
            for _, fn, waits, inc, amt in steps:
                for sem, val in waits:
                    eng.wait_ge(sem.h, val)
                inst = fn(eng)
                if inc is not None:
                    assert inst is not None, "inc step returned no instruction"
                    inst.then_inc(inc.h, amt)


def build(cfg, nsteps=None):
    nseq, S, D, NH, DH, E, H, NR = (cfg["nseq"], cfg["S"], cfg["D"], cfg["NH"],
                                    cfg["DH"], cfg["E"], cfg["H"], cfg["NRANGE"])
    assert DH == 64 and H == 128 and D % 128 == 0 and NH % 2 == 0
    NP = S // 128
    DC = D // 128
    RQ = S // NR
    RC = RQ // 128
    assert RC % 2 == 0 and RQ % 512 == 0 and NP % 4 == 0
    assert NH * DH == D
    HP = NH // 2
    NTOK = nseq * S
    ES = S

    nc = bass.Bass("TRN2", target_bir_lowering=False)

    x_d = nc.dram_tensor("x", [NTOK, D], F32, kind="ExternalInput")
    wq_d = nc.dram_tensor("wq", [NH, D, DH], F32, kind="ExternalInput")
    wk_d = nc.dram_tensor("wk", [NH, D, DH], F32, kind="ExternalInput")
    wv_d = nc.dram_tensor("wv", [NH, D, DH], F32, kind="ExternalInput")
    wo_d = nc.dram_tensor("wo", [D, D], F32, kind="ExternalInput")
    selw_d = nc.dram_tensor("selw", [D, E], F32, kind="ExternalInput")
    keys_d = nc.dram_tensor("keys", [E, D, H], F32, kind="ExternalInput")
    vals_d = nc.dram_tensor("vals", [E, H, D], F32, kind="ExternalInput")
    ident_d = nc.dram_tensor("ident", [128, 128], F32, kind="ExternalInput")
    mask_d = nc.dram_tensor("mask", [128, 128], F32, kind="ExternalInput")
    out_d = nc.dram_tensor("out", [NTOK, D], F32, kind="ExternalOutput")

    x_rows = x_d.ap().rearrange("(n p) d -> n p d", p=128)
    out_rows = out_d.ap().rearrange("(n p) d -> n p d", p=128)

    sch = Sched()
    with ExitStack() as ctx:
        sb = lambda name, shape: ctx.enter_context(nc.sbuf_tensor(name, shape, F32))
        ps = lambda name, shape: ctx.enter_context(nc.psum_tensor(name, shape, F32))

        xn = sb("xn", [128, DC * S])
        zf = sb("zf", [128, HP * RQ])
        _sizes_attn = [("kS", S), ("qS", RQ), ("vT", S), ("En", 4 * ES),
                       ("EnT", 1024), ("wbuf", 2 * 3 * DC * 128),
                       ("normB", RQ, 2)]
        _sizes_out = [("hsb", E * 512), ("kbuf", 3 * DC * H),
                      ("vbuf", 2 * E * 128), ("wobuf", 2 * D),
                      ("selB", 512), ("outst", DC * 512),
                      ("outtok", 2 * D), ("xres", 2 * D)]
        _sizes_ln = [("xtile", 2 * D), ("xsq", D), ("xntok", 4 * D),
                     ("selstg", 512), ("seltmp", 512), ("selw", DC * E)]
        POOLC = max(sum(e[1] for e in grp)
                    for grp in (_sizes_attn, _sizes_out, _sizes_ln))
        pool = sb("pool", [128, POOLC])
        _views = {}
        for grp in (_sizes_attn, _sizes_out, _sizes_ln):
            off = 0
            for entry in grp:
                nm, ncols = entry[0], entry[1]
                npart = entry[2] if len(entry) > 2 else 128
                _views[nm] = View(pool, off, ncols, npart)
                off += ncols
        kS, qS, vT, En, EnT, wbuf, normB = (
            _views[n] for n in ("kS", "qS", "vT", "En", "EnT", "wbuf",
                                "normB"))
        hsb, kbuf, vbuf, wobuf, selB, outst, outtok, xres = (
            _views[n] for n in ("hsb", "kbuf", "vbuf", "wobuf", "selB",
                                "outst", "outtok", "xres"))
        xtile, xsq, xntok, selstg, seltmp, selw = (
            _views[n] for n in ("xtile", "xsq", "xntok", "selstg", "seltmp",
                                "selw"))
        vfm = View(pool, EnT.c0, 512)
        nsq = View(pool, En.c0 + 2 * ES, S)
        nsq2 = View(pool, En.c0 + 3 * ES, S)
        ident = sb("ident_sb", [128, 128])
        mask = sb("mask_sb", [128, 128])
        sel = sb("sel", [128, S])
        ones2 = sb("ones2", [128, 2])
        lps = sb("lps", [128, 16])
        lrc = sb("lrc", [128, 12])
        lncols = sb("lncols", [128, 8])
        cst = sb("cst", [128, 3])
        onesr = sb("onesr", [128, 128])

        psA = [ps("psA0", [128, 512]), ps("psA1", [128, 512])]
        psB = [ps("psB0", [128, 512]), ps("psB1", [128, 512])]
        psT = [ps("psT0", [128, 512]), ps("psT1", [128, 512])]
        psZ = [ps("psZ0", [128, 512]), ps("psZ1", [128, 512])]

        qc = Sem(ctx, nc, "qc")
        qx = [Sem(ctx, nc, "qx0"), Sem(ctx, nc, "qx1")]
        qw = [Sem(ctx, nc, "qw0"), Sem(ctx, nc, "qw1")]
        qkb = [Sem(ctx, nc, "qkb0"), Sem(ctx, nc, "qkb1"),
               Sem(ctx, nc, "qkb2")]
        qvb = [Sem(ctx, nc, "qvb0"), Sem(ctx, nc, "qvb1")]
        qwo = [Sem(ctx, nc, "qwo0"), Sem(ctx, nc, "qwo1")]
        qxr = [Sem(ctx, nc, "qxr0"), Sem(ctx, nc, "qxr1")]
        qsB = [Sem(ctx, nc, "qsB0"), Sem(ctx, nc, "qsB1")]
        stS = [Sem(ctx, nc, "st0"), Sem(ctx, nc, "st1")]
        pe = Sem(ctx, nc, "pe")
        dv = Sem(ctx, nc, "dv")
        ac = Sem(ctx, nc, "ac")
        allsems = ([qc] + qx + qw + qkb + qvb + qwo + qxr + qsB + stS
                   + [pe, dv, ac])

        def snap():
            return [(s, s.v) for s in allsems]

        def barrier():
            w = snap()
            for e in ENGINES:
                sch.op(e, lambda eng: eng.nop(), waits=w)

        def dma_w(queue_eng, out_ap, in_ap, sem, waits=()):
            return sch.op(queue_eng,
                          lambda eng, o=out_ap, i=in_ap: eng.dma_start(o, i),
                          waits=waits, inc=sem, amt=16)

        def copy_step(eng_name, out_ap, in_ap, waits):
            sem = ac if eng_name == "scalar" else dv
            if eng_name == "scalar":
                fn = lambda eng, o=out_ap, i=in_ap: eng.activation(o, i,
                                                                   AF.Copy)
            else:
                fn = lambda eng, o=out_ap, i=in_ap: eng.tensor_copy(o, i)
            return sem, sch.op(eng_name, fn, waits=waits, inc=sem)

        dma_w("sync", ident[:, :], ident_d.ap(), qc)
        cst_qs = dma_w("sync", mask[:, :], mask_d.ap(), qc)
        selw_in = bass.AP(selw_d, 0, [[E, 128], [128 * E, DC], [1, E]])
        def init_fn(eng):
            eng.memset(ones2[:, :], 0.0)
            eng.drain()
            eng.memset(ones2[0:64, 0:1], 1.0)
            eng.drain()
            eng.memset(ones2[64:128, 1:2], 1.0)
            eng.drain()
            eng.memset(cst[:, 0:1], 1e-5)
            eng.drain()
            eng.memset(cst[:, 1:2], float(math.log(0.125)))
            eng.drain()
            eng.memset(cst[:, 2:3], -12.0)
            eng.drain()
            eng.memset(onesr[32:33, :], 1.0)
            eng.drain()
            r2 = eng.memset(onesr[64:65, :], 1.0)
            eng.drain()
            return r2
        sch.op("vector", init_fn, inc=dv)

        def load_pair_w(hp_i, buf, waits):
            base = buf * 3 * DC * 128
            last = None
            for mi, wd in enumerate((wq_d, wk_d, wv_d)):
                for hh in range(2):
                    head = 2 * hp_i + hh
                    in_ap = bass.AP(wd, head * D * DH,
                                    [[DH, 128], [128 * DH, DC], [1, DH]])
                    ob = wbuf[:, base + mi * DC * 128 + hh * DH:
                              base + mi * DC * 128 + hh * DH + 1]
                    out_ap = bass.AP(ob.tensor, ob.offset,
                                     [list(ob.ap[0]), [128, DC], [1, DH]])
                    last = dma_w("sync", out_ap, in_ap, qw[buf], waits)
            return last

        def wpair(buf, mi, c):
            cb = buf * 3 * DC * 128 + mi * DC * 128 + c * 128
            return wbuf[:, cb:cb + 128]

        for sq in range(nseq):
            selw_qs = dma_w("sync", selw[:, 0:DC * E], selw_in, qc, snap())
            ln_xt_guard = [snap(), snap()]
            xntok_guard = snap()
            psT_guard = [snap(), snap()]
            for g in range(NP // 4):
                for tc4 in range(4):
                    t = 4 * g + tc4
                    b = t % 2
                    xv = dma_w("sync", xtile[:, b * D:(b + 1) * D],
                               x_rows[sq * NP + t], qx[b], ln_xt_guard[b])
                    xt = xtile[:, b * D:(b + 1) * D]
                    s1 = lncols[:, 0:1]
                    s2 = lncols[:, 1:2]
                    mcol = lncols[:, 2:3]
                    msq = lncols[:, 3:4]
                    varc = lncols[:, 4:5]
                    rlnc = lncols[:, 5:6]
                    rcol = lncols[:, 6:7]
                    sch.op("vector", lambda eng, a=s1, i=xt:
                           (eng.reduce_sum(a, i, axis=AX.X), eng.drain())[0],
                           waits=[(qx[b], xv)], inc=dv)
                    acv = sch.op("scalar", lambda eng, i=xt, a=s2:
                                 (eng.activation(xsq[:, :], i, AF.Square,
                                                 accum_out=a), eng.drain())[0],
                                 waits=[(qx[b], xv)], inc=ac)
                    dvv = sch.op("vector", lambda eng, a=mcol, i=s1, m2=msq,
                                 vc=varc, b2=s2:
                                 (eng.tensor_scalar_mul(a, i, 1.0 / D),
                                  eng.drain(),
                                  eng.tensor_mul(m2, a, a),
                                  eng.drain(),
                                  eng.tensor_scalar(vc, b2, 1.0 / D, m2,
                                                    op0=ALU.mult,
                                                    op1=ALU.subtract),
                                  eng.drain())[-2],
                                 waits=[(ac, acv)], inc=dv)
                    acv = sch.op("scalar", lambda eng, rl=rlnc, vc=varc, rc=rcol:
                                 (eng.activation(rl, vc, AF.Ln, bias=cst[:, 0:1]),
                                  eng.drain(),
                                  eng.activation(rc, rl, AF.Exp, scale=-0.5),
                                  eng.drain())[-2],
                                 waits=[(dv, dvv)], inc=ac)
                    dvv = sch.op("vector", lambda eng, tc4=tc4, i=xt, a=mcol,
                                 rc=rcol:
                                 eng.tensor_scalar(
                                     xntok[:, tc4 * D:(tc4 + 1) * D], i, a, rc,
                                     op0=ALU.subtract, op1=ALU.mult),
                                 waits=[(ac, acv)] + xntok_guard, inc=dv)
                    ln_xt_guard[b] = [(dv, dvv), (ac, acv)]
                xn_dv = dv.v
                last_pe = 0
                for d in range(DC):
                    bb = d % 2

                    def tfn(eng, d=d, bb=bb):
                        r = None
                        for tc4 in range(4):
                            r = eng.transpose(
                                psT[bb][:, tc4 * 128:(tc4 + 1) * 128],
                                xntok[:, tc4 * D + d * 128:
                                      tc4 * D + d * 128 + 128],
                                ident[:, :])
                        return r
                    pv = sch.op("tensor", tfn,
                                waits=[(dv, xn_dv), (qc, cst_qs)] + psT_guard[bb],
                                inc=pe)
                    last_pe = pv
                    _, av = copy_step("scalar",
                                      xn[:, d * S + g * 512:d * S + g * 512 + 512],
                                      psT[bb][:, :], [(pe, pv)])
                    psT_guard[bb] = [(ac, av)]
                xntok_guard = [(pe, last_pe)]

                def self_fn(eng, g=g):
                    r = None
                    for c in range(DC):
                        r = eng.matmul(psA[0][0:E, :],
                                       selw[:, c * E:(c + 1) * E],
                                       xn[:, c * S + g * 512:
                                          c * S + g * 512 + 512],
                                       start=(c == 0), stop=(c == DC - 1))
                    return r
                pv = sch.op("tensor", self_fn,
                            waits=[(ac, ac.v), (qc, selw_qs)], inc=pe)
                av = sch.op("scalar", lambda eng:
                            eng.activation(seltmp[0:E, 0:512], psA[0][0:E, :],
                                           AF.Exp, scale=-1.0),
                            waits=[(pe, pv)], inc=ac)
                sch.op("vector", lambda eng, g=g:
                       (eng.tensor_scalar_add(selstg[0:E, 0:512], seltmp[0:E, 0:512], 1.0),
                        eng.drain(),
                        eng.reciprocal(sel[0:E, g * 512:g * 512 + 512],
                                       selstg[0:E, 0:512]),
                        eng.drain())[-2],
                       waits=[(ac, av)], inc=dv)
            barrier()

            for r in range(NR):
                PCH = RC * (r + 1)
                PTOK = 128 * PCH
                wload = [0, 0]
                wload[0] = load_pair_w(0, 0, ())
                if HP > 1:
                    wload[1] = load_pair_w(1, 1, ())
                for hp in range(HP):
                    b = hp % 2
                    wv_ready = wload[b]
                    projA_guard = [snap(), snap()]
                    projB_guard = [snap(), snap()]
                    psTg = [snap(), snap()]
                    kq_ready = 0
                    vt_pe = 0
                    for piece in range((PTOK + 511) // 512):
                        w = min(512, PTOK - piece * 512)
                        pb = piece % 2

                        def kfn(eng, pb=pb, piece=piece, w=w, b=b):
                            r2 = None
                            for c in range(DC):
                                r2 = eng.matmul(
                                    psA[pb][:, 0:w], wpair(b, 1, c),
                                    xn[:, c * S + piece * 512:
                                       c * S + piece * 512 + w],
                                    start=(c == 0), stop=(c == DC - 1))
                            return r2
                        pv = sch.op("tensor", kfn,
                                    waits=[(qw[b], wv_ready)] + projA_guard[pb],
                                    inc=pe)
                        _, av = copy_step("scalar",
                                          kS[:, piece * 512:piece * 512 + w],
                                          psA[pb][:, 0:w], [(pe, pv)])
                        projA_guard[pb] = [(ac, av)]
                        kq_ready = av

                        def vfn(eng, pb=pb, piece=piece, w=w, b=b):
                            r2 = None
                            for c in range(DC):
                                r2 = eng.matmul(
                                    psB[pb][:, 0:w], wpair(b, 2, c),
                                    xn[:, c * S + piece * 512:
                                       c * S + piece * 512 + w],
                                    start=(c == 0), stop=(c == DC - 1))
                            return r2
                        pv = sch.op("tensor", vfn,
                                    waits=[(qw[b], wv_ready)] + projB_guard[pb],
                                    inc=pe)
                        _, av = copy_step("scalar", vfm[:, 0:w],
                                          psB[pb][:, 0:w],
                                          [(pe, pv), (pe, vt_pe)])
                        projB_guard[pb] = [(ac, av)]

                        def vtfn(eng, w=w):
                            r2 = None
                            for u in range(w // 128):
                                r2 = eng.transpose(
                                    psT[0][:, u * 128:u * 128 + 128],
                                    vfm[:, u * 128:u * 128 + 128],
                                    ident[:, :])
                            return r2
                        pv = sch.op("tensor", vtfn,
                                    waits=[(ac, av)] + psTg[0], inc=pe)
                        vt_pe = pv
                        _, av2 = copy_step("scalar",
                                           vT[:, piece * 512:piece * 512 + w],
                                           psT[0][:, 0:w], [(pe, pv)])
                        psTg[0] = [(ac, av2)]
                    q_ready = 0
                    for piece in range(RQ // 512):
                        pb = piece % 2

                        def qfn(eng, pb=pb, piece=piece, b=b, r=r):
                            r2 = None
                            for c in range(DC):
                                r2 = eng.matmul(
                                    psA[pb][:, :], wpair(b, 0, c),
                                    xn[:, c * S + r * RQ + piece * 512:
                                       c * S + r * RQ + piece * 512 + 512],
                                    start=(c == 0), stop=(c == DC - 1))
                            return r2
                        pv = sch.op("tensor", qfn,
                                    waits=[(qw[b], wv_ready)] + projA_guard[pb],
                                    inc=pe)
                        _, av = copy_step("scalar",
                                          qS[:, piece * 512:piece * 512 + 512],
                                          psA[pb][:, :], [(pe, pv)])
                        projA_guard[pb] = [(ac, av)]
                        q_ready = av
                    if hp + 1 < HP and hp + 1 >= 2:
                        wload[(hp + 1) % 2] = load_pair_w(hp + 1, (hp + 1) % 2,
                                                          [(pe, pe.v)])
                    mneg = ac.v
                    sA_guard = [snap(), snap()]
                    sB_guard = [snap(), snap()]
                    enT_guard = [snap(), snap()]
                    zps_guard = [snap(), snap()]
                    psTg = [snap(), snap()]
                    box = dict(en_slot_guard=snap(), exp_done={})

                    def gen_S(sct):
                        exp_done = {}
                        for par in range(2):
                            cc = 2 * sct + par
                            c = RC * r + cc
                            pe_c = 128 * (c + 1)
                            npc = (pe_c + 511) // 512
                            for j in range(npc):
                                w = min(512, pe_c - j * 512)
                                jb = j % 2
                                pv = sch.op(
                                    "tensor",
                                    lambda eng, cc=cc, j=j, w=w, jb=jb:
                                    (eng.matmul(psA[jb][:, 0:w],
                                                qS[0:64, cc * 128:cc * 128 + 128],
                                                kS[0:64, j * 512:j * 512 + w],
                                                start=True, stop=True,
                                                tile_position=(0, 0)),
                                     eng.matmul(psB[jb][:, 0:w],
                                                qS[64:128, cc * 128:cc * 128 + 128],
                                                kS[64:128, j * 512:j * 512 + w],
                                                start=True, stop=True,
                                                tile_position=(64, 0)))[-1],
                                    waits=[(ac, mneg)]
                                    + sA_guard[jb] + sB_guard[jb],
                                    inc=pe)
                                yield "S"
                                mwait = []
                                if j == npc - 1:
                                    dvv = sch.op(
                                        "vector",
                                        lambda eng, jb=jb, w=w:
                                        (eng.tensor_add(psA[jb][:, w - 128:w],
                                                        psA[jb][:, w - 128:w],
                                                        mask[:, :]),
                                         eng.drain(),
                                         eng.tensor_add(psB[jb][:, w - 128:w],
                                                        psB[jb][:, w - 128:w],
                                                        mask[:, :]),
                                         eng.drain())[-2],
                                        waits=[(pe, pv)], inc=dv)
                                    mwait = [(dv, dvv)]
                                yield "EXPS"
                                for hh, pX in ((0, psA), (1, psB)):
                                    slot = ES * (2 * hh + par)
                                    lcol = (hh * 2 + par) * 4 + j
                                    acv = sch.op(
                                        "scalar",
                                        lambda eng, pX=pX, jb=jb, j=j, w=w,
                                        cc=cc, hh=hh, slot=slot, lcol=lcol:
                                        eng.activation(
                                            En[:, slot + j * 512:
                                               slot + j * 512 + w],
                                            pX[jb][:, 0:w], AF.Exp,
                                            bias=cst[:, 2:3],
                                            scale=0.125,
                                            accum_out=lps[:, lcol:lcol + 1]),
                                        waits=[(pe, pv)] + mwait
                                        + box["en_slot_guard"],
                                        inc=ac)
                                    if hh == 0:
                                        sA_guard[jb] = [(ac, acv)]
                                    else:
                                        sB_guard[jb] = [(ac, acv)]
                                yield "S"
                            for hh in range(2):
                                slot = ES * (2 * hh + par)
                                base = (hh * 2 + par) * 4
                                sl = hh * 2 + par
                                tail = []
                                if par == 0:
                                    zv = sch.op("vector",
                                                lambda eng, slot=slot,
                                                pe_c=pe_c:
                                                eng.memset(
                                                    En[:, slot + pe_c:
                                                       slot + pe_c + 128],
                                                    0.0),
                                                waits=[(ac, ac.v)], inc=dv)
                                    tail = [(dv, zv)]
                                dvv = sch.op(
                                    "vector",
                                    lambda eng, base=base, npc=npc, slot=slot,
                                    pe_c=pe_c, sl=sl:
                                    (eng.reduce_sum(lrc[:, sl:sl + 1],
                                                    lps[:, base:base + npc],
                                                    axis=AX.X),
                                     eng.drain(),
                                     eng.reciprocal(lrc[:, 4 + sl:5 + sl],
                                                    lrc[:, sl:sl + 1]),
                                     eng.drain(),
                                     eng.tensor_scalar_mul(
                                         En[:, slot:slot + pe_c],
                                         En[:, slot:slot + pe_c],
                                         lrc[:, 4 + sl:5 + sl]),
                                     eng.drain())[-2],
                                    waits=[(ac, ac.v)] + tail, inc=dv)
                                exp_done[(hh, par)] = dvv
                                yield "S"
                        box["exp_done"] = exp_done

                    def gen_TZ(sct):
                        exp_done = box["exp_done"]
                        n_i = RC * r + 2 * sct + 2
                        sb_z = sct % 2
                        en_wait = [(dv, exp_done[(h2, p2)])
                                   for h2 in range(2) for p2 in range(2)]
                        cvs = [None, None]

                        def z_op(i):
                            tb = i % 2
                            csem, cv = cvs[tb]
                            zv = sch.op("tensor",
                                        lambda eng, i=i, tb=tb, sb_z=sb_z,
                                        n_i=n_i:
                                        eng.matmul(psZ[sb_z][:, :],
                                                   vT[:, i * 128:(i + 1) * 128],
                                                   EnT[:, tb * 512:
                                                       (tb + 1) * 512],
                                                   start=(i == 0),
                                                   stop=(i == n_i - 1)),
                                        waits=[(csem, cv)] + zps_guard[sb_z],
                                        inc=pe)
                            enT_guard[tb] = [(pe, zv)]

                        for i in range(n_i):
                            tb = i % 2

                            def trfn(eng, i=i, tb=tb):
                                r2 = None
                                for h2 in range(2):
                                    for p2 in range(2):
                                        slot = ES * (2 * h2 + p2)
                                        col = (2 * h2 + p2) * 128
                                        r2 = eng.transpose(
                                            psT[tb][:, col:col + 128],
                                            En[:, slot + i * 128:
                                               slot + i * 128 + 128],
                                            ident[:, :])
                                return r2
                            pv = sch.op("tensor", trfn,
                                        waits=en_wait + psTg[tb], inc=pe)
                            if i == n_i - 1:
                                box["en_slot_guard"] = [(pe, pv)]
                            yield "T"
                            ceng = "scalar" if i % 2 == 0 else "vector"
                            csem, cv = copy_step(
                                ceng, EnT[:, tb * 512:(tb + 1) * 512],
                                psT[tb][:, :],
                                [(pe, pv)] + enT_guard[tb])
                            psTg[tb] = [(csem, cv)]
                            cvs[tb] = (csem, cv)
                            if i >= 1:
                                z_op(i - 1)
                            yield "Z"
                        z_op(n_i - 1)
                        av = sch.op("scalar",
                                    lambda eng, sb_z=sb_z, hp=hp, sct=sct:
                                    (eng.activation(
                                        zf[0:64, hp * RQ + sct * 256:
                                           hp * RQ + sct * 256 + 256],
                                        psZ[sb_z][0:64, 0:256], AF.Copy),
                                     eng.drain(),
                                     eng.activation(
                                         zf[64:128, hp * RQ + sct * 256:
                                            hp * RQ + sct * 256 + 256],
                                         psZ[sb_z][64:128, 256:512], AF.Copy),
                                     eng.drain())[-2],
                                    waits=[(pe, pe.v)], inc=ac)
                        zps_guard[sb_z] = [(ac, av)]

                    tz = None
                    for sct in range(RC // 2):
                        lead = 0
                        for item in gen_S(sct):
                            if item == "EXPS" and tz is not None:
                                for _ in tz:
                                    pass
                                tz = None
                            elif tz is not None:
                                lead += 1
                                if lead > 3:
                                    next(tz, None)
                        if tz is not None:
                            for _ in tz:
                                pass
                        tz = gen_TZ(sct)
                    if tz is not None:
                        for _ in tz:
                            pass
                barrier()
                for p2 in range(RQ // 512):
                    rtok0 = r * RQ + p2 * 512
                    hA_guard = [snap(), snap()]
                    kbuf_guard = [snap(), snap(), snap()]
                    sel_ready = snap()
                    psTb_guard = [snap(), snap()]
                    stage_guard = [snap(), snap()]
                    selB_guard1 = snap()
                    h_dv = 0
                    for e in range(E):
                        eb = e % 2
                        ke = e % 3
                        srow = 32 * (eb + 1)
                        in_ap = bass.AP(keys_d, e * D * H,
                                        [[H, 128], [128 * H, DC], [1, H]])
                        kb_ready = dma_w("sync",
                                         kbuf[:, ke * DC * H:(ke + 1) * DC * H],
                                         in_ap, qkb[ke], kbuf_guard[ke])
                        v0 = sch.op("gpsimd",
                                    lambda eng, e=e, srow=srow, rtok0=rtok0:
                                    eng.dma_start(
                                        sel[srow:srow + 1,
                                            rtok0:rtok0 + 512],
                                        sel[e:e + 1, rtok0:rtok0 + 512]),
                                    waits=(sel_ready if e < 2 else [])
                                    + stage_guard[eb],
                                    inc=qsB[eb], amt=16)
                        bv = sch.op("tensor",
                                    lambda eng, eb=eb, srow=srow,
                                    rtok0=rtok0:
                                    eng.matmul(psT[eb][:, 0:512],
                                               onesr[srow:srow + 1, 0:128],
                                               sel[srow:srow + 1,
                                                   rtok0:rtok0 + 512],
                                               start=True, stop=True),
                                    waits=[(qsB[eb], v0)] + psTb_guard[eb],
                                    inc=pe)
                        stage_guard[eb] = [(pe, bv)]
                        cv = sch.op("vector",
                                    lambda eng, eb=eb:
                                    eng.tensor_copy(selB[:, 0:512],
                                                    psT[eb][:, 0:512]),
                                    waits=[(pe, bv)] + selB_guard1,
                                    inc=dv)
                        psTb_guard[eb] = [(dv, cv)]

                        def hfn(eng, eb=eb, ke=ke, rtok0=rtok0):
                            r2 = None
                            for c in range(DC):
                                r2 = eng.matmul(
                                    psA[eb][:, :],
                                    kbuf[:, ke * DC * H + c * H:
                                         ke * DC * H + c * H + 128],
                                    xn[:, c * S + rtok0:c * S + rtok0 + 512],
                                    start=(c == 0), stop=(c == DC - 1))
                            return r2
                        pv = sch.op("tensor", hfn,
                                    waits=[(qkb[ke], kb_ready)] + hA_guard[eb],
                                    inc=pe)
                        h_dv = sch.op("vector",
                                      lambda eng, e=e, eb=eb:
                                      eng.scalar_tensor_tensor(
                                          hsb[:, e * 512:(e + 1) * 512],
                                          psA[eb][:, :], 0.0,
                                          selB[:, 0:512],
                                          op0=ALU.max, op1=ALU.mult),
                                      waits=[(pe, pv), (dv, cv)],
                                      inc=dv)
                        hA_guard[eb] = [(dv, h_dv)]
                        selB_guard1 = [(dv, h_dv)]
                        kbuf_guard[ke] = [(pe, pv)]
                    oB_guard = [snap(), snap()]
                    wo_guard = [snap(), snap()]
                    vb_guard = [snap(), snap()]
                    outst_av = 0
                    for d in range(DC):
                        db = d % 2
                        wo_in = bass.AP(wo_d, d * 128,
                                        [[D, 128], [128 * D, DC], [1, 128]])
                        wo_ready = dma_w("sync",
                                         wobuf[:, db * D:(db + 1) * D],
                                         wo_in, qwo[db], wo_guard[db])
                        v_in = bass.AP(vals_d, d * 128,
                                       [[D, 128], [H * D, E], [1, 128]])
                        vb_ready = dma_w("sync",
                                         vbuf[:, db * E * 128:
                                              (db + 1) * E * 128],
                                         v_in, qvb[db], vb_guard[db])

                        def ofn(eng, db=db, p2=p2):
                            r2 = None
                            for fc in range(HP):
                                r2 = eng.matmul(
                                    psB[db][:, :],
                                    wobuf[:, db * D + fc * 128:
                                          db * D + fc * 128 + 128],
                                    zf[:, fc * RQ + p2 * 512:
                                       fc * RQ + p2 * 512 + 512],
                                    start=(fc == 0), stop=False)
                            for e2 in range(E):
                                r2 = eng.matmul(
                                    psB[db][:, :],
                                    vbuf[:, db * E * 128 + e2 * 128:
                                         db * E * 128 + e2 * 128 + 128],
                                    hsb[:, e2 * 512:(e2 + 1) * 512],
                                    start=False, stop=(e2 == E - 1))
                            return r2
                        pv = sch.op("tensor", ofn,
                                    waits=[(qvb[db], vb_ready),
                                           (qwo[db], wo_ready),
                                           (dv, h_dv)] + oB_guard[db],
                                    inc=pe)
                        _, outst_av = copy_step("scalar",
                                                outst[:, d * 512:(d + 1) * 512],
                                                psB[db][:, :], [(pe, pv)])
                        oB_guard[db] = [(ac, outst_av)]
                        wo_guard[db] = [(pe, pv)]
                        vb_guard[db] = [(pe, pv)]
                    xr_guard = [snap(), snap()]
                    ot_guard = [snap(), snap()]
                    psTo_guard = snap()
                    for tcp in range(4):
                        tb = tcp % 2
                        t_global = sq * NP + (rtok0 + tcp * 128) // 128
                        xr_ready = dma_w("sync", xres[:, tb * D:(tb + 1) * D],
                                         x_rows[t_global], qxr[tb],
                                         xr_guard[tb])

                        def otfn(eng, tcp=tcp):
                            r2 = None
                            for d2 in range(DC):
                                dst = psT[0] if d2 < 4 else psT[1]
                                r2 = eng.transpose(
                                    dst[:, (d2 % 4) * 128:(d2 % 4) * 128 + 128],
                                    outst[:, d2 * 512 + tcp * 128:
                                          d2 * 512 + tcp * 128 + 128],
                                    ident[:, :])
                            return r2
                        pv = sch.op("tensor", otfn,
                                    waits=[(ac, outst_av)] + psTo_guard,
                                    inc=pe)

                        def resfn(eng, tb=tb):
                            w1 = min(DC, 4) * 128
                            a1 = eng.tensor_add(
                                outtok[:, tb * D:tb * D + w1],
                                psT[0][:, 0:w1],
                                xres[:, tb * D:tb * D + w1])
                            eng.drain()
                            if DC > 4:
                                w2 = (DC - 4) * 128
                                a1 = eng.tensor_add(
                                    outtok[:, tb * D + 512:tb * D + 512 + w2],
                                    psT[1][:, 0:w2],
                                    xres[:, tb * D + 512:tb * D + 512 + w2])
                                eng.drain()
                            return a1
                        dvv = sch.op("vector", resfn,
                                     waits=[(pe, pv), (qxr[tb], xr_ready)]
                                     + ot_guard[tb],
                                     inc=dv)
                        psTo_guard = [(dv, dvv)]
                        xr_guard[tb] = [(dv, dvv)]
                        sv = sch.op("gpsimd",
                                    lambda eng, tb=tb, t_global=t_global:
                                    eng.dma_start(out_rows[t_global],
                                                  outtok[:, tb * D:
                                                         (tb + 1) * D]),
                                    waits=[(dv, dvv)], inc=stS[tb], amt=16)
                        ot_guard[tb] = [(stS[tb], sv)]
                    barrier()
            barrier()

        f0, f1 = stS[0].v, stS[1].v
        sch.op("gpsimd", lambda eng, a=f0, b2=f1:
               (eng.wait_ge(stS[0].h, a), eng.wait_ge(stS[1].h, b2))[0])

        if nsteps is not None:
            print(f"[build] total steps {len(sch.steps)}, emitting {nsteps}")
            sch.steps = sch.steps[:nsteps]
        with nc.Block() as block:
            sch.emit(block)
    return nc


def _host_layout(inputs, n_cores):
    x = np.ascontiguousarray(np.asarray(inputs["L0_attn_x"], dtype=np.float32))
    B, S, D = x.shape
    shards = np.split(x.reshape(B * S, D), n_cores, axis=0)
    wq = np.ascontiguousarray(np.asarray(inputs["L0_attn_W_Q"],
                                         dtype=np.float32).transpose(0, 2, 1))
    wk = np.ascontiguousarray(np.asarray(inputs["L0_attn_W_K"],
                                         dtype=np.float32).transpose(0, 2, 1))
    wv = np.ascontiguousarray(np.asarray(inputs["L0_attn_W_V"],
                                         dtype=np.float32).transpose(0, 2, 1))
    wo = np.ascontiguousarray(np.asarray(inputs["L0_attn_W_O"],
                                         dtype=np.float32).T)
    selw = np.ascontiguousarray(np.asarray(inputs["L0_ffn_expert_sel"],
                                           dtype=np.float32).T)
    keys = np.ascontiguousarray(np.asarray(inputs["L0_ffn_keys"],
                                           dtype=np.float32))
    vals = np.ascontiguousarray(np.asarray(inputs["L0_ffn_values"],
                                           dtype=np.float32))
    ident = np.eye(128, dtype=np.float32)
    mask = np.where(np.arange(128)[None, :] <= np.arange(128)[:, None],
                    np.float32(0.0), np.float32(-1e9)).astype(np.float32)
    in_maps = []
    for c in range(n_cores):
        in_maps.append({"x": np.ascontiguousarray(shards[c]), "wq": wq,
                        "wk": wk, "wv": wv, "wo": wo, "selw": selw,
                        "keys": keys, "vals": vals, "ident": ident,
                        "mask": mask})
    return in_maps, (B, S, D)


def kernel(L0_attn_x, L0_attn_W_K, L0_attn_W_Q, L0_attn_W_V, L0_attn_W_O,
           L0_ffn_expert_sel, L0_ffn_keys, L0_ffn_values):
    inputs = dict(L0_attn_x=L0_attn_x, L0_attn_W_K=L0_attn_W_K,
                  L0_attn_W_Q=L0_attn_W_Q, L0_attn_W_V=L0_attn_W_V,
                  L0_attn_W_O=L0_attn_W_O, L0_ffn_expert_sel=L0_ffn_expert_sel,
                  L0_ffn_keys=L0_ffn_keys, L0_ffn_values=L0_ffn_values)
    n_cores = 8
    in_maps, (B, S, D) = _host_layout(inputs, n_cores)
    nc = build(FULL_CFG)
    res = run_bass_kernel_spmd(nc, in_maps, core_ids=list(range(n_cores)))
    out = np.concatenate([np.asarray(r["out"]) for r in res.results], axis=0)
    return out.reshape(B, S, D)


# version 19 (final)
# speedup vs baseline: 1.0549x; 1.0549x over previous
import math
from contextlib import ExitStack

import numpy as np

import concourse.bass as bass
import concourse.mybir as mybir
from concourse.bass_utils import run_bass_kernel_spmd

F32 = mybir.dt.float32
AF = mybir.ActivationFunctionType
ALU = mybir.AluOpType
AX = mybir.AxisListType

FULL_CFG = dict(nseq=2, S=2048, D=1024, NH=16, DH=64, E=16, H=128, NRANGE=2)

ENGINES = ("sync", "gpsimd", "tensor", "vector", "scalar")


class Sem:
    def __init__(self, ctx, nc, name):
        self.h = ctx.enter_context(nc.semaphore(name))
        self.v = 0


class View:

    def __init__(self, t, c0, ncols, nparts=128):
        self.t, self.c0, self.n = t, c0, ncols
        self.np = nparts

    def __getitem__(self, idx):
        rows, cols = idx
        if rows == slice(None):
            rows = slice(0, self.np)
        a = self.c0 + (cols.start or 0)
        b = self.c0 + (cols.stop if cols.stop is not None else self.n)
        return self.t[rows, a:b]


class Sched:

    def __init__(self):
        self.steps = []

    def op(self, eng, fn, waits=(), inc=None, amt=1):
        w = {}
        for s, v in waits:
            if v > 0:
                w[s] = max(w.get(s, 0), v)
        if inc is not None:
            inc.v += amt
        self.steps.append((eng, fn, tuple(w.items()), inc, amt))
        return None if inc is None else inc.v

    def emit(self, block):
        for name in ENGINES:
            self._emitter(block, name)

    def _emitter(self, block, name):
        dec = getattr(block, name)
        steps = [s for s in self.steps if s[0] == name]

        @dec
        def _(eng):
            for _, fn, waits, inc, amt in steps:
                for sem, val in waits:
                    eng.wait_ge(sem.h, val)
                inst = fn(eng)
                if inc is not None:
                    assert inst is not None, "inc step returned no instruction"
                    inst.then_inc(inc.h, amt)


def build(cfg, nsteps=None):
    nseq, S, D, NH, DH, E, H, NR = (cfg["nseq"], cfg["S"], cfg["D"], cfg["NH"],
                                    cfg["DH"], cfg["E"], cfg["H"], cfg["NRANGE"])
    assert DH == 64 and H == 128 and D % 128 == 0 and NH % 2 == 0
    NP = S // 128
    DC = D // 128
    RQ = S // NR
    RC = RQ // 128
    assert RC % 2 == 0 and RQ % 512 == 0 and NP % 4 == 0
    assert NH * DH == D
    HP = NH // 2
    NTOK = nseq * S
    ES = S

    nc = bass.Bass("TRN2", target_bir_lowering=False)

    x_d = nc.dram_tensor("x", [NTOK, D], F32, kind="ExternalInput")
    wq_d = nc.dram_tensor("wq", [NH, D, DH], F32, kind="ExternalInput")
    wk_d = nc.dram_tensor("wk", [NH, D, DH], F32, kind="ExternalInput")
    wv_d = nc.dram_tensor("wv", [NH, D, DH], F32, kind="ExternalInput")
    wo_d = nc.dram_tensor("wo", [D, D], F32, kind="ExternalInput")
    selw_d = nc.dram_tensor("selw", [D, E], F32, kind="ExternalInput")
    keys_d = nc.dram_tensor("keys", [E, D, H], F32, kind="ExternalInput")
    vals_d = nc.dram_tensor("vals", [E, H, D], F32, kind="ExternalInput")
    ident_d = nc.dram_tensor("ident", [128, 128], F32, kind="ExternalInput")
    mask_d = nc.dram_tensor("mask", [128, 128], F32, kind="ExternalInput")
    out_d = nc.dram_tensor("out", [NTOK, D], F32, kind="ExternalOutput")

    x_rows = x_d.ap().rearrange("(n p) d -> n p d", p=128)
    out_rows = out_d.ap().rearrange("(n p) d -> n p d", p=128)

    sch = Sched()
    with ExitStack() as ctx:
        sb = lambda name, shape: ctx.enter_context(nc.sbuf_tensor(name, shape, F32))
        ps = lambda name, shape: ctx.enter_context(nc.psum_tensor(name, shape, F32))

        xn = sb("xn", [128, DC * S])
        zf = sb("zf", [128, HP * RQ])
        _sizes_attn = [("kS", S), ("qS", RQ), ("vT", S), ("En", 4 * ES),
                       ("EnT", 1024), ("wbuf", 2 * 3 * DC * 128),
                       ("normB", RQ, 2)]
        _sizes_out = [("hsb", E * 512), ("kbuf", 3 * DC * H),
                      ("vbuf", 2 * E * 128), ("wobuf", 2 * D),
                      ("selB", 512), ("outst", DC * 512),
                      ("outtok", 2 * D), ("xres", 2 * D)]
        _sizes_ln = [("xtile", 2 * D), ("xsq", D), ("xntok", 4 * D),
                     ("selstg", 512), ("seltmp", 512), ("selw", DC * E)]
        POOLC = max(sum(e[1] for e in grp)
                    for grp in (_sizes_attn, _sizes_out, _sizes_ln))
        pool = sb("pool", [128, POOLC])
        _views = {}
        for grp in (_sizes_attn, _sizes_out, _sizes_ln):
            off = 0
            for entry in grp:
                nm, ncols = entry[0], entry[1]
                npart = entry[2] if len(entry) > 2 else 128
                _views[nm] = View(pool, off, ncols, npart)
                off += ncols
        kS, qS, vT, En, EnT, wbuf, normB = (
            _views[n] for n in ("kS", "qS", "vT", "En", "EnT", "wbuf",
                                "normB"))
        hsb, kbuf, vbuf, wobuf, selB, outst, outtok, xres = (
            _views[n] for n in ("hsb", "kbuf", "vbuf", "wobuf", "selB",
                                "outst", "outtok", "xres"))
        xtile, xsq, xntok, selstg, seltmp, selw = (
            _views[n] for n in ("xtile", "xsq", "xntok", "selstg", "seltmp",
                                "selw"))
        vfm = View(pool, EnT.c0, 512)
        nsq = View(pool, En.c0 + 2 * ES, S)
        nsq2 = View(pool, En.c0 + 3 * ES, S)
        ident = sb("ident_sb", [128, 128])
        mask = sb("mask_sb", [128, 128])
        sel = sb("sel", [128, S])
        ones2 = sb("ones2", [128, 2])
        lps = sb("lps", [128, 16])
        lrc = sb("lrc", [128, 12])
        lncols = sb("lncols", [128, 8])
        cst = sb("cst", [128, 3])
        onesr = sb("onesr", [128, 128])

        psA = [ps("psA0", [128, 512]), ps("psA1", [128, 512])]
        psB = [ps("psB0", [128, 512]), ps("psB1", [128, 512])]
        psT = [ps("psT0", [128, 512]), ps("psT1", [128, 512])]
        psZ = [ps("psZ0", [128, 512]), ps("psZ1", [128, 512])]

        qc = Sem(ctx, nc, "qc")
        qx = [Sem(ctx, nc, "qx0"), Sem(ctx, nc, "qx1")]
        qw = [Sem(ctx, nc, "qw0"), Sem(ctx, nc, "qw1")]
        qkb = [Sem(ctx, nc, "qkb0"), Sem(ctx, nc, "qkb1"),
               Sem(ctx, nc, "qkb2")]
        qvb = [Sem(ctx, nc, "qvb0"), Sem(ctx, nc, "qvb1")]
        qwo = [Sem(ctx, nc, "qwo0"), Sem(ctx, nc, "qwo1")]
        qxr = [Sem(ctx, nc, "qxr0"), Sem(ctx, nc, "qxr1")]
        qsB = [Sem(ctx, nc, "qsB0"), Sem(ctx, nc, "qsB1")]
        stS = [Sem(ctx, nc, "st0"), Sem(ctx, nc, "st1")]
        pe = Sem(ctx, nc, "pe")
        dv = Sem(ctx, nc, "dv")
        ac = Sem(ctx, nc, "ac")
        allsems = ([qc] + qx + qw + qkb + qvb + qwo + qxr + qsB + stS
                   + [pe, dv, ac])

        def snap():
            return [(s, s.v) for s in allsems]

        def barrier():
            w = snap()
            for e in ENGINES:
                sch.op(e, lambda eng: eng.nop(), waits=w)

        def dma_w(queue_eng, out_ap, in_ap, sem, waits=()):
            return sch.op(queue_eng,
                          lambda eng, o=out_ap, i=in_ap: eng.dma_start(o, i),
                          waits=waits, inc=sem, amt=16)

        def copy_step(eng_name, out_ap, in_ap, waits):
            sem = ac if eng_name == "scalar" else dv
            if eng_name == "scalar":
                fn = lambda eng, o=out_ap, i=in_ap: eng.activation(o, i,
                                                                   AF.Copy)
            else:
                fn = lambda eng, o=out_ap, i=in_ap: eng.tensor_copy(o, i)
            return sem, sch.op(eng_name, fn, waits=waits, inc=sem)

        dma_w("sync", ident[:, :], ident_d.ap(), qc)
        cst_qs = dma_w("sync", mask[:, :], mask_d.ap(), qc)
        selw_in = bass.AP(selw_d, 0, [[E, 128], [128 * E, DC], [1, E]])
        def init_fn(eng):
            eng.memset(ones2[:, :], 0.0)
            eng.drain()
            eng.memset(ones2[0:64, 0:1], 1.0)
            eng.drain()
            eng.memset(ones2[64:128, 1:2], 1.0)
            eng.drain()
            eng.memset(cst[:, 0:1], 1e-5)
            eng.drain()
            eng.memset(cst[:, 1:2], float(math.log(0.125)))
            eng.drain()
            eng.memset(cst[:, 2:3], -12.0)
            eng.drain()
            eng.memset(onesr[32:33, :], 1.0)
            eng.drain()
            r2 = eng.memset(onesr[64:65, :], 1.0)
            eng.drain()
            return r2
        sch.op("vector", init_fn, inc=dv)

        def load_pair_w(hp_i, buf, waits):
            base = buf * 3 * DC * 128
            last = None
            for mi, wd in enumerate((wq_d, wk_d, wv_d)):
                for hh in range(2):
                    head = 2 * hp_i + hh
                    in_ap = bass.AP(wd, head * D * DH,
                                    [[DH, 128], [128 * DH, DC], [1, DH]])
                    ob = wbuf[:, base + mi * DC * 128 + hh * DH:
                              base + mi * DC * 128 + hh * DH + 1]
                    out_ap = bass.AP(ob.tensor, ob.offset,
                                     [list(ob.ap[0]), [128, DC], [1, DH]])
                    last = dma_w("sync", out_ap, in_ap, qw[buf], waits)
            return last

        def wpair(buf, mi, c):
            cb = buf * 3 * DC * 128 + mi * DC * 128 + c * 128
            return wbuf[:, cb:cb + 128]

        for sq in range(nseq):
            selw_qs = dma_w("sync", selw[:, 0:DC * E], selw_in, qc, snap())
            ln_xt_guard = [snap(), snap()]
            xntok_guard = snap()
            psT_guard = [snap(), snap()]
            for g in range(NP // 4):
                for tc4 in range(4):
                    t = 4 * g + tc4
                    b = t % 2
                    xv = dma_w("sync", xtile[:, b * D:(b + 1) * D],
                               x_rows[sq * NP + t], qx[b], ln_xt_guard[b])
                    xt = xtile[:, b * D:(b + 1) * D]
                    s1 = lncols[:, 0:1]
                    s2 = lncols[:, 1:2]
                    mcol = lncols[:, 2:3]
                    msq = lncols[:, 3:4]
                    varc = lncols[:, 4:5]
                    rlnc = lncols[:, 5:6]
                    rcol = lncols[:, 6:7]
                    sch.op("vector", lambda eng, a=s1, i=xt:
                           (eng.reduce_sum(a, i, axis=AX.X), eng.drain())[0],
                           waits=[(qx[b], xv)], inc=dv)
                    acv = sch.op("scalar", lambda eng, i=xt, a=s2:
                                 (eng.activation(xsq[:, :], i, AF.Square,
                                                 accum_out=a), eng.drain())[0],
                                 waits=[(qx[b], xv)], inc=ac)
                    dvv = sch.op("vector", lambda eng, a=mcol, i=s1, m2=msq,
                                 vc=varc, b2=s2:
                                 (eng.tensor_scalar_mul(a, i, 1.0 / D),
                                  eng.drain(),
                                  eng.tensor_mul(m2, a, a),
                                  eng.drain(),
                                  eng.tensor_scalar(vc, b2, 1.0 / D, m2,
                                                    op0=ALU.mult,
                                                    op1=ALU.subtract),
                                  eng.drain())[-2],
                                 waits=[(ac, acv)], inc=dv)
                    acv = sch.op("scalar", lambda eng, rl=rlnc, vc=varc, rc=rcol:
                                 (eng.activation(rl, vc, AF.Ln, bias=cst[:, 0:1]),
                                  eng.drain(),
                                  eng.activation(rc, rl, AF.Exp, scale=-0.5),
                                  eng.drain())[-2],
                                 waits=[(dv, dvv)], inc=ac)
                    dvv = sch.op("vector", lambda eng, tc4=tc4, i=xt, a=mcol,
                                 rc=rcol:
                                 eng.tensor_scalar(
                                     xntok[:, tc4 * D:(tc4 + 1) * D], i, a, rc,
                                     op0=ALU.subtract, op1=ALU.mult),
                                 waits=[(ac, acv)] + xntok_guard, inc=dv)
                    ln_xt_guard[b] = [(dv, dvv), (ac, acv)]
                xn_dv = dv.v
                last_pe = 0
                for d in range(DC):
                    bb = d % 2

                    def tfn(eng, d=d, bb=bb):
                        r = None
                        for tc4 in range(4):
                            r = eng.transpose(
                                psT[bb][:, tc4 * 128:(tc4 + 1) * 128],
                                xntok[:, tc4 * D + d * 128:
                                      tc4 * D + d * 128 + 128],
                                ident[:, :])
                        return r
                    pv = sch.op("tensor", tfn,
                                waits=[(dv, xn_dv), (qc, cst_qs)] + psT_guard[bb],
                                inc=pe)
                    last_pe = pv
                    _, av = copy_step("scalar",
                                      xn[:, d * S + g * 512:d * S + g * 512 + 512],
                                      psT[bb][:, :], [(pe, pv)])
                    psT_guard[bb] = [(ac, av)]
                xntok_guard = [(pe, last_pe)]

                def self_fn(eng, g=g):
                    r = None
                    for c in range(DC):
                        r = eng.matmul(psA[0][0:E, :],
                                       selw[:, c * E:(c + 1) * E],
                                       xn[:, c * S + g * 512:
                                          c * S + g * 512 + 512],
                                       start=(c == 0), stop=(c == DC - 1))
                    return r
                pv = sch.op("tensor", self_fn,
                            waits=[(ac, ac.v), (qc, selw_qs)], inc=pe)
                av = sch.op("scalar", lambda eng:
                            eng.activation(seltmp[0:E, 0:512], psA[0][0:E, :],
                                           AF.Exp, scale=-1.0),
                            waits=[(pe, pv)], inc=ac)
                sch.op("vector", lambda eng, g=g:
                       (eng.tensor_scalar_add(selstg[0:E, 0:512], seltmp[0:E, 0:512], 1.0),
                        eng.drain(),
                        eng.reciprocal(sel[0:E, g * 512:g * 512 + 512],
                                       selstg[0:E, 0:512]),
                        eng.drain())[-2],
                       waits=[(ac, av)], inc=dv)
            barrier()

            for r in range(NR):
                PCH = RC * (r + 1)
                PTOK = 128 * PCH
                wload = [0, 0]
                wload[0] = load_pair_w(0, 0, ())
                if HP > 1:
                    wload[1] = load_pair_w(1, 1, ())
                for hp in range(HP):
                    b = hp % 2
                    wv_ready = wload[b]
                    projA_guard = [snap(), snap()]
                    projB_guard = [snap(), snap()]
                    psTg = [snap(), snap()]
                    kq_ready = 0
                    vt_pe = 0
                    for piece in range((PTOK + 511) // 512):
                        w = min(512, PTOK - piece * 512)
                        pb = piece % 2

                        def kfn(eng, pb=pb, piece=piece, w=w, b=b):
                            r2 = None
                            for c in range(DC):
                                r2 = eng.matmul(
                                    psA[pb][:, 0:w], wpair(b, 1, c),
                                    xn[:, c * S + piece * 512:
                                       c * S + piece * 512 + w],
                                    start=(c == 0), stop=(c == DC - 1))
                            return r2
                        pv = sch.op("tensor", kfn,
                                    waits=[(qw[b], wv_ready)] + projA_guard[pb],
                                    inc=pe)
                        _, av = copy_step("scalar",
                                          kS[:, piece * 512:piece * 512 + w],
                                          psA[pb][:, 0:w], [(pe, pv)])
                        projA_guard[pb] = [(ac, av)]
                        kq_ready = av

                        def vfn(eng, pb=pb, piece=piece, w=w, b=b):
                            r2 = None
                            for c in range(DC):
                                r2 = eng.matmul(
                                    psB[pb][:, 0:w], wpair(b, 2, c),
                                    xn[:, c * S + piece * 512:
                                       c * S + piece * 512 + w],
                                    start=(c == 0), stop=(c == DC - 1))
                            return r2
                        pv = sch.op("tensor", vfn,
                                    waits=[(qw[b], wv_ready)] + projB_guard[pb],
                                    inc=pe)
                        _, av = copy_step("scalar", vfm[:, 0:w],
                                          psB[pb][:, 0:w],
                                          [(pe, pv), (pe, vt_pe)])
                        projB_guard[pb] = [(ac, av)]

                        def vtfn(eng, w=w):
                            r2 = None
                            for u in range(w // 128):
                                r2 = eng.transpose(
                                    psT[0][:, u * 128:u * 128 + 128],
                                    vfm[:, u * 128:u * 128 + 128],
                                    ident[:, :])
                            return r2
                        pv = sch.op("tensor", vtfn,
                                    waits=[(ac, av)] + psTg[0], inc=pe)
                        vt_pe = pv
                        _, av2 = copy_step("scalar",
                                           vT[:, piece * 512:piece * 512 + w],
                                           psT[0][:, 0:w], [(pe, pv)])
                        psTg[0] = [(ac, av2)]
                    q_ready = 0
                    for piece in range(RQ // 512):
                        pb = piece % 2

                        def qfn(eng, pb=pb, piece=piece, b=b, r=r):
                            r2 = None
                            for c in range(DC):
                                r2 = eng.matmul(
                                    psA[pb][:, :], wpair(b, 0, c),
                                    xn[:, c * S + r * RQ + piece * 512:
                                       c * S + r * RQ + piece * 512 + 512],
                                    start=(c == 0), stop=(c == DC - 1))
                            return r2
                        pv = sch.op("tensor", qfn,
                                    waits=[(qw[b], wv_ready)] + projA_guard[pb],
                                    inc=pe)
                        _, av = copy_step("scalar",
                                          qS[:, piece * 512:piece * 512 + 512],
                                          psA[pb][:, :], [(pe, pv)])
                        projA_guard[pb] = [(ac, av)]
                        q_ready = av
                    if hp + 1 < HP and hp + 1 >= 2:
                        wload[(hp + 1) % 2] = load_pair_w(hp + 1, (hp + 1) % 2,
                                                          [(pe, pe.v)])
                    mneg = ac.v
                    sA_guard = [snap(), snap()]
                    sB_guard = [snap(), snap()]
                    enT_guard = [snap(), snap()]
                    zps_guard = [snap(), snap()]
                    psTg = [snap(), snap()]
                    box = dict(en_slot_guard=snap(), exp_done={})

                    def gen_S(sct):
                        exp_done = {}
                        for par in range(2):
                            cc = 2 * sct + par
                            c = RC * r + cc
                            pe_c = 128 * (c + 1)
                            npc = (pe_c + 511) // 512
                            for j in range(npc):
                                w = min(512, pe_c - j * 512)
                                jb = j % 2
                                pv = sch.op(
                                    "tensor",
                                    lambda eng, cc=cc, j=j, w=w, jb=jb:
                                    (eng.matmul(psA[jb][:, 0:w],
                                                qS[0:64, cc * 128:cc * 128 + 128],
                                                kS[0:64, j * 512:j * 512 + w],
                                                start=True, stop=True,
                                                tile_position=(0, 0)),
                                     eng.matmul(psB[jb][:, 0:w],
                                                qS[64:128, cc * 128:cc * 128 + 128],
                                                kS[64:128, j * 512:j * 512 + w],
                                                start=True, stop=True,
                                                tile_position=(64, 0)))[-1],
                                    waits=[(ac, mneg)]
                                    + sA_guard[jb] + sB_guard[jb],
                                    inc=pe)
                                yield "S"
                                mwait = []
                                if j == npc - 1:
                                    dvv = sch.op(
                                        "vector",
                                        lambda eng, jb=jb, w=w:
                                        (eng.tensor_add(psA[jb][:, w - 128:w],
                                                        psA[jb][:, w - 128:w],
                                                        mask[:, :]),
                                         eng.drain(),
                                         eng.tensor_add(psB[jb][:, w - 128:w],
                                                        psB[jb][:, w - 128:w],
                                                        mask[:, :]),
                                         eng.drain())[-2],
                                        waits=[(pe, pv)], inc=dv)
                                    mwait = [(dv, dvv)]
                                yield "EXPS"
                                for hh, pX in ((0, psA), (1, psB)):
                                    slot = ES * (2 * hh + par)
                                    lcol = (hh * 2 + par) * 4 + j
                                    acv = sch.op(
                                        "scalar",
                                        lambda eng, pX=pX, jb=jb, j=j, w=w,
                                        cc=cc, hh=hh, slot=slot, lcol=lcol:
                                        eng.activation(
                                            En[:, slot + j * 512:
                                               slot + j * 512 + w],
                                            pX[jb][:, 0:w], AF.Exp,
                                            bias=cst[:, 2:3],
                                            scale=0.125,
                                            accum_out=lps[:, lcol:lcol + 1]),
                                        waits=[(pe, pv)] + mwait
                                        + box["en_slot_guard"],
                                        inc=ac)
                                    if hh == 0:
                                        sA_guard[jb] = [(ac, acv)]
                                    else:
                                        sB_guard[jb] = [(ac, acv)]
                                yield "S"
                            for hh in range(2):
                                slot = ES * (2 * hh + par)
                                base = (hh * 2 + par) * 4
                                sl = hh * 2 + par
                                tail = []
                                if par == 0:
                                    zv = sch.op("vector",
                                                lambda eng, slot=slot,
                                                pe_c=pe_c:
                                                eng.memset(
                                                    En[:, slot + pe_c:
                                                       slot + pe_c + 128],
                                                    0.0),
                                                waits=[(ac, ac.v)], inc=dv)
                                    tail = [(dv, zv)]
                                dvv = sch.op(
                                    "vector",
                                    lambda eng, base=base, npc=npc, slot=slot,
                                    pe_c=pe_c, sl=sl:
                                    (eng.reduce_sum(lrc[:, sl:sl + 1],
                                                    lps[:, base:base + npc],
                                                    axis=AX.X),
                                     eng.drain(),
                                     eng.reciprocal(lrc[:, 4 + sl:5 + sl],
                                                    lrc[:, sl:sl + 1]),
                                     eng.drain(),
                                     eng.tensor_scalar_mul(
                                         En[:, slot:slot + pe_c],
                                         En[:, slot:slot + pe_c],
                                         lrc[:, 4 + sl:5 + sl]),
                                     eng.drain())[-2],
                                    waits=[(ac, ac.v)] + tail, inc=dv)
                                exp_done[(hh, par)] = dvv
                                yield "S"
                        box["exp_done"] = exp_done

                    def gen_TZ(sct):
                        exp_done = box["exp_done"]
                        n_i = RC * r + 2 * sct + 2
                        sb_z = sct % 2
                        en_wait = [(dv, exp_done[(h2, p2)])
                                   for h2 in range(2) for p2 in range(2)]
                        cvs = [None, None]

                        def z_op(i):
                            tb = i % 2
                            csem, cv = cvs[tb]
                            zv = sch.op("tensor",
                                        lambda eng, i=i, tb=tb, sb_z=sb_z,
                                        n_i=n_i:
                                        eng.matmul(psZ[sb_z][:, :],
                                                   vT[:, i * 128:(i + 1) * 128],
                                                   EnT[:, tb * 512:
                                                       (tb + 1) * 512],
                                                   start=(i == 0),
                                                   stop=(i == n_i - 1)),
                                        waits=[(csem, cv)] + zps_guard[sb_z],
                                        inc=pe)
                            enT_guard[tb] = [(pe, zv)]

                        for i in range(n_i):
                            tb = i % 2

                            def trfn(eng, i=i, tb=tb):
                                r2 = None
                                for h2 in range(2):
                                    for p2 in range(2):
                                        slot = ES * (2 * h2 + p2)
                                        col = (2 * h2 + p2) * 128
                                        r2 = eng.transpose(
                                            psT[tb][:, col:col + 128],
                                            En[:, slot + i * 128:
                                               slot + i * 128 + 128],
                                            ident[:, :])
                                return r2
                            pv = sch.op("tensor", trfn,
                                        waits=en_wait + psTg[tb], inc=pe)
                            if i == n_i - 1:
                                box["en_slot_guard"] = [(pe, pv)]
                            yield "T"
                            ceng = "scalar" if i % 2 == 0 else "vector"
                            csem, cv = copy_step(
                                ceng, EnT[:, tb * 512:(tb + 1) * 512],
                                psT[tb][:, :],
                                [(pe, pv)] + enT_guard[tb])
                            psTg[tb] = [(csem, cv)]
                            cvs[tb] = (csem, cv)
                            if i >= 1:
                                z_op(i - 1)
                            yield "Z"
                        z_op(n_i - 1)
                        av = sch.op("scalar",
                                    lambda eng, sb_z=sb_z, hp=hp, sct=sct:
                                    (eng.activation(
                                        zf[0:64, hp * RQ + sct * 256:
                                           hp * RQ + sct * 256 + 256],
                                        psZ[sb_z][0:64, 0:256], AF.Copy),
                                     eng.drain(),
                                     eng.activation(
                                         zf[64:128, hp * RQ + sct * 256:
                                            hp * RQ + sct * 256 + 256],
                                         psZ[sb_z][64:128, 256:512], AF.Copy),
                                     eng.drain())[-2],
                                    waits=[(pe, pe.v)], inc=ac)
                        zps_guard[sb_z] = [(ac, av)]

                    tz = None
                    for sct in range(RC // 2):
                        lead = 0
                        for item in gen_S(sct):
                            if item == "EXPS" and tz is not None:
                                for _ in tz:
                                    pass
                                tz = None
                            elif tz is not None:
                                lead += 1
                                if lead > 6:
                                    next(tz, None)
                        if tz is not None:
                            for _ in tz:
                                pass
                        tz = gen_TZ(sct)
                    if tz is not None:
                        for _ in tz:
                            pass
                barrier()
                for p2 in range(RQ // 512):
                    rtok0 = r * RQ + p2 * 512
                    hA_guard = [snap(), snap()]
                    kbuf_guard = [snap(), snap(), snap()]
                    sel_ready = snap()
                    psTb_guard = [snap(), snap()]
                    stage_guard = [snap(), snap()]
                    selB_guard1 = snap()
                    h_dv = 0
                    for e in range(E):
                        eb = e % 2
                        ke = e % 3
                        srow = 32 * (eb + 1)
                        in_ap = bass.AP(keys_d, e * D * H,
                                        [[H, 128], [128 * H, DC], [1, H]])
                        kb_ready = dma_w("sync",
                                         kbuf[:, ke * DC * H:(ke + 1) * DC * H],
                                         in_ap, qkb[ke], kbuf_guard[ke])
                        v0 = sch.op("gpsimd",
                                    lambda eng, e=e, srow=srow, rtok0=rtok0:
                                    eng.dma_start(
                                        sel[srow:srow + 1,
                                            rtok0:rtok0 + 512],
                                        sel[e:e + 1, rtok0:rtok0 + 512]),
                                    waits=(sel_ready if e < 2 else [])
                                    + stage_guard[eb],
                                    inc=qsB[eb], amt=16)
                        bv = sch.op("tensor",
                                    lambda eng, eb=eb, srow=srow,
                                    rtok0=rtok0:
                                    eng.matmul(psT[eb][:, 0:512],
                                               onesr[srow:srow + 1, 0:128],
                                               sel[srow:srow + 1,
                                                   rtok0:rtok0 + 512],
                                               start=True, stop=True),
                                    waits=[(qsB[eb], v0)] + psTb_guard[eb],
                                    inc=pe)
                        stage_guard[eb] = [(pe, bv)]
                        cv = sch.op("vector",
                                    lambda eng, eb=eb:
                                    eng.tensor_copy(selB[:, 0:512],
                                                    psT[eb][:, 0:512]),
                                    waits=[(pe, bv)] + selB_guard1,
                                    inc=dv)
                        psTb_guard[eb] = [(dv, cv)]

                        def hfn(eng, eb=eb, ke=ke, rtok0=rtok0):
                            r2 = None
                            for c in range(DC):
                                r2 = eng.matmul(
                                    psA[eb][:, :],
                                    kbuf[:, ke * DC * H + c * H:
                                         ke * DC * H + c * H + 128],
                                    xn[:, c * S + rtok0:c * S + rtok0 + 512],
                                    start=(c == 0), stop=(c == DC - 1))
                            return r2
                        pv = sch.op("tensor", hfn,
                                    waits=[(qkb[ke], kb_ready)] + hA_guard[eb],
                                    inc=pe)
                        h_dv = sch.op("vector",
                                      lambda eng, e=e, eb=eb:
                                      eng.scalar_tensor_tensor(
                                          hsb[:, e * 512:(e + 1) * 512],
                                          psA[eb][:, :], 0.0,
                                          selB[:, 0:512],
                                          op0=ALU.max, op1=ALU.mult),
                                      waits=[(pe, pv), (dv, cv)],
                                      inc=dv)
                        hA_guard[eb] = [(dv, h_dv)]
                        selB_guard1 = [(dv, h_dv)]
                        kbuf_guard[ke] = [(pe, pv)]
                    oB_guard = [snap(), snap()]
                    wo_guard = [snap(), snap()]
                    vb_guard = [snap(), snap()]
                    outst_av = 0
                    for d in range(DC):
                        db = d % 2
                        wo_in = bass.AP(wo_d, d * 128,
                                        [[D, 128], [128 * D, DC], [1, 128]])
                        wo_ready = dma_w("sync",
                                         wobuf[:, db * D:(db + 1) * D],
                                         wo_in, qwo[db], wo_guard[db])
                        v_in = bass.AP(vals_d, d * 128,
                                       [[D, 128], [H * D, E], [1, 128]])
                        vb_ready = dma_w("sync",
                                         vbuf[:, db * E * 128:
                                              (db + 1) * E * 128],
                                         v_in, qvb[db], vb_guard[db])

                        def ofn(eng, db=db, p2=p2):
                            r2 = None
                            for fc in range(HP):
                                r2 = eng.matmul(
                                    psB[db][:, :],
                                    wobuf[:, db * D + fc * 128:
                                          db * D + fc * 128 + 128],
                                    zf[:, fc * RQ + p2 * 512:
                                       fc * RQ + p2 * 512 + 512],
                                    start=(fc == 0), stop=False)
                            for e2 in range(E):
                                r2 = eng.matmul(
                                    psB[db][:, :],
                                    vbuf[:, db * E * 128 + e2 * 128:
                                         db * E * 128 + e2 * 128 + 128],
                                    hsb[:, e2 * 512:(e2 + 1) * 512],
                                    start=False, stop=(e2 == E - 1))
                            return r2
                        pv = sch.op("tensor", ofn,
                                    waits=[(qvb[db], vb_ready),
                                           (qwo[db], wo_ready),
                                           (dv, h_dv)] + oB_guard[db],
                                    inc=pe)
                        _, outst_av = copy_step("scalar",
                                                outst[:, d * 512:(d + 1) * 512],
                                                psB[db][:, :], [(pe, pv)])
                        oB_guard[db] = [(ac, outst_av)]
                        wo_guard[db] = [(pe, pv)]
                        vb_guard[db] = [(pe, pv)]
                    xr_guard = [snap(), snap()]
                    ot_guard = [snap(), snap()]
                    psTo_guard = snap()
                    for tcp in range(4):
                        tb = tcp % 2
                        t_global = sq * NP + (rtok0 + tcp * 128) // 128
                        xr_ready = dma_w("sync", xres[:, tb * D:(tb + 1) * D],
                                         x_rows[t_global], qxr[tb],
                                         xr_guard[tb])

                        def otfn(eng, tcp=tcp):
                            r2 = None
                            for d2 in range(DC):
                                dst = psT[0] if d2 < 4 else psT[1]
                                r2 = eng.transpose(
                                    dst[:, (d2 % 4) * 128:(d2 % 4) * 128 + 128],
                                    outst[:, d2 * 512 + tcp * 128:
                                          d2 * 512 + tcp * 128 + 128],
                                    ident[:, :])
                            return r2
                        pv = sch.op("tensor", otfn,
                                    waits=[(ac, outst_av)] + psTo_guard,
                                    inc=pe)

                        def resfn(eng, tb=tb):
                            w1 = min(DC, 4) * 128
                            a1 = eng.tensor_add(
                                outtok[:, tb * D:tb * D + w1],
                                psT[0][:, 0:w1],
                                xres[:, tb * D:tb * D + w1])
                            eng.drain()
                            if DC > 4:
                                w2 = (DC - 4) * 128
                                a1 = eng.tensor_add(
                                    outtok[:, tb * D + 512:tb * D + 512 + w2],
                                    psT[1][:, 0:w2],
                                    xres[:, tb * D + 512:tb * D + 512 + w2])
                                eng.drain()
                            return a1
                        dvv = sch.op("vector", resfn,
                                     waits=[(pe, pv), (qxr[tb], xr_ready)]
                                     + ot_guard[tb],
                                     inc=dv)
                        psTo_guard = [(dv, dvv)]
                        xr_guard[tb] = [(dv, dvv)]
                        sv = sch.op("gpsimd",
                                    lambda eng, tb=tb, t_global=t_global:
                                    eng.dma_start(out_rows[t_global],
                                                  outtok[:, tb * D:
                                                         (tb + 1) * D]),
                                    waits=[(dv, dvv)], inc=stS[tb], amt=16)
                        ot_guard[tb] = [(stS[tb], sv)]
                    barrier()
            barrier()

        f0, f1 = stS[0].v, stS[1].v
        sch.op("gpsimd", lambda eng, a=f0, b2=f1:
               (eng.wait_ge(stS[0].h, a), eng.wait_ge(stS[1].h, b2))[0])

        if nsteps is not None:
            print(f"[build] total steps {len(sch.steps)}, emitting {nsteps}")
            sch.steps = sch.steps[:nsteps]
        with nc.Block() as block:
            sch.emit(block)
    return nc


def _host_layout(inputs, n_cores):
    x = np.ascontiguousarray(np.asarray(inputs["L0_attn_x"], dtype=np.float32))
    B, S, D = x.shape
    shards = np.split(x.reshape(B * S, D), n_cores, axis=0)
    wq = np.ascontiguousarray(np.asarray(inputs["L0_attn_W_Q"],
                                         dtype=np.float32).transpose(0, 2, 1))
    wk = np.ascontiguousarray(np.asarray(inputs["L0_attn_W_K"],
                                         dtype=np.float32).transpose(0, 2, 1))
    wv = np.ascontiguousarray(np.asarray(inputs["L0_attn_W_V"],
                                         dtype=np.float32).transpose(0, 2, 1))
    wo = np.ascontiguousarray(np.asarray(inputs["L0_attn_W_O"],
                                         dtype=np.float32).T)
    selw = np.ascontiguousarray(np.asarray(inputs["L0_ffn_expert_sel"],
                                           dtype=np.float32).T)
    keys = np.ascontiguousarray(np.asarray(inputs["L0_ffn_keys"],
                                           dtype=np.float32))
    vals = np.ascontiguousarray(np.asarray(inputs["L0_ffn_values"],
                                           dtype=np.float32))
    ident = np.eye(128, dtype=np.float32)
    mask = np.where(np.arange(128)[None, :] <= np.arange(128)[:, None],
                    np.float32(0.0), np.float32(-1e9)).astype(np.float32)
    in_maps = []
    for c in range(n_cores):
        in_maps.append({"x": np.ascontiguousarray(shards[c]), "wq": wq,
                        "wk": wk, "wv": wv, "wo": wo, "selw": selw,
                        "keys": keys, "vals": vals, "ident": ident,
                        "mask": mask})
    return in_maps, (B, S, D)


def kernel(L0_attn_x, L0_attn_W_K, L0_attn_W_Q, L0_attn_W_V, L0_attn_W_O,
           L0_ffn_expert_sel, L0_ffn_keys, L0_ffn_values):
    inputs = dict(L0_attn_x=L0_attn_x, L0_attn_W_K=L0_attn_W_K,
                  L0_attn_W_Q=L0_attn_W_Q, L0_attn_W_V=L0_attn_W_V,
                  L0_attn_W_O=L0_attn_W_O, L0_ffn_expert_sel=L0_ffn_expert_sel,
                  L0_ffn_keys=L0_ffn_keys, L0_ffn_values=L0_ffn_values)
    n_cores = 8
    in_maps, (B, S, D) = _host_layout(inputs, n_cores)
    nc = build(FULL_CFG)
    res = run_bass_kernel_spmd(nc, in_maps, core_ids=list(range(n_cores)))
    out = np.concatenate([np.asarray(r["out"]) for r in res.results], axis=0)
    return out.reshape(B, S, D)
